# version 07
# baseline (speedup 1.0000x reference)
import math
from contextlib import ExitStack

import numpy as np
import ml_dtypes
import concourse.bass as bass
import concourse.mybir as mybir
from concourse.bass_utils import run_bass_kernel_spmd

F32 = mybir.dt.float32
BF16 = mybir.dt.bfloat16
AF = mybir.ActivationFunctionType
ALU = mybir.AluOpType
AX = mybir.AxisListType
BF = ml_dtypes.bfloat16

SCALE = 1.0 / math.sqrt(128.0)
ENGS = ("pe", "act", "dve", "sp", "gp")

_NC_CACHE = {}


class IR:

    def __init__(self):
        self.items = {e: [] for e in ENGS}
        self.cnt = {}

    def val(self, sem):
        return self.cnt.get(sem, 0)

    def emit(self, eng, fn, waits=(), inc=None):
        ws = tuple((s, v) for (s, v) in waits if v is not None and v > 0)
        self.items[eng].append((ws, fn, inc))
        if inc is not None:
            sem, k = inc
            self.cnt[sem] = self.cnt.get(sem, 0) + k
            return (sem, self.cnt[sem])
        return None


def _wt(*tokens):
    best = {}
    for t in tokens:
        if t is None:
            continue
        s, v = t
        if v is not None and (s not in best or v > best[s]):
            best[s] = v
    return [(s, v) for s, v in best.items()]


def build_nc():
    nc = bass.Bass("TRN2", target_bir_lowering=False)

    xT = nc.dram_tensor("xT", [4096, 1024], F32, kind="ExternalInput")
    wq = nc.dram_tensor("wq", [32 * 4096, 128], F32, kind="ExternalInput")
    wkv = nc.dram_tensor("wkv", [16 * 4096, 128], F32, kind="ExternalInput")
    wo = nc.dram_tensor("wo", [32 * 4096, 128], F32, kind="ExternalInput")
    wb = nc.dram_tensor("wb", [32 * 4096, 128], F32, kind="ExternalInput")
    art = nc.dram_tensor("art", [4096, 40], F32, kind="ExternalInput")
    bwt = nc.dram_tensor("bwt", [16, 4096], F32, kind="ExternalInput")
    coresst = nc.dram_tensor("coresst", [128, 16], F32, kind="ExternalInput")
    bb = nc.dram_tensor("bb", [128, 32], F32, kind="ExternalInput")
    cosT = nc.dram_tensor("cosT", [128, 1024], F32, kind="ExternalInput")
    sinST = nc.dram_tensor("sinST", [128, 1024], F32, kind="ExternalInput")
    tri = nc.dram_tensor("tri", [128, 512], BF16, kind="ExternalInput")
    identb = nc.dram_tensor("identb", [128, 128], BF16, kind="ExternalInput")
    identf = nc.dram_tensor("identf", [128, 128], F32, kind="ExternalInput")
    rep16 = nc.dram_tensor("rep16", [16, 128], BF16, kind="ExternalInput")
    sel8 = nc.dram_tensor("sel8", [8, 128], BF16, kind="ExternalInput")
    outT = nc.dram_tensor("outT", [4096, 1024], F32, kind="ExternalOutput")
    kag_in = nc.dram_tensor("kag_in", [2048, 1024], BF16)
    kag_out = nc.dram_tensor("kag_out", [8192, 1024], BF16)
    gbuf = nc.dram_tensor("gbuf", [4096, 1024], BF16)

    ir = IR()

    with ExitStack() as st:
        def sb(name, shape, dt):
            return st.enter_context(nc.sbuf_tensor(name, shape, dt))

        def psum(name, shape, dt):
            return st.enter_context(nc.psum_tensor(name, shape, dt))

        xn = sb("xn", [128, 32 * 1024], BF16)
        ctx = sb("ctx", [128, 32 * 1024], BF16)
        kvar = sb("kvar", [128, 8192], BF16)
        wb0 = sb("wb0", [128, 4096], BF16)
        wb1 = sb("wb1", [128, 4096], BF16)
        q0 = sb("q0", [128, 1024], BF16)
        q1 = sb("q1", [128, 1024], BF16)
        p0 = sb("p0", [128, 1024], BF16)
        p1 = sb("p1", [128, 1024], BF16)
        cosb = sb("cosb", [128, 1024], F32)
        sinb = sb("sinb", [128, 1024], F32)
        trib = sb("trib", [128, 512], BF16)
        idb = sb("idb", [128, 128], BF16)
        idf = sb("idf", [128, 128], F32)
        artb = sb("artb", [128, 32 * 40], BF16)
        coresb = sb("coresb", [128, 16], BF16)
        bbb = sb("bbb", [128, 32], F32)
        ra = sb("ra", [1, 512], F32)
        rb = sb("rb", [1, 512], F32)
        rc = sb("rc", [1, 512], F32)
        lbb = sb("lbb", [128, 512], F32)
        axb = sb("axb", [16, 1024], BF16)
        rlb = sb("rlb", [8, 512], BF16)
        ptm = sb("ptm", [128, 32], BF16)
        pfb = sb("pfb", [8, 512], BF16)
        prb = sb("prb", [128, 512], BF16)
        axpb = sb("axpb", [128, 512], BF16)
        mixb = sb("mixb", [16, 512], BF16)
        sq0 = sb("sq0", [128, 512], BF16)
        sq1 = sb("sq1", [128, 512], BF16)
        onesb = sb("onesb", [128, 1], BF16)
        onesf = sb("onesf", [1, 128], F32)
        b16 = sb("b16", [128, 1], F32)
        epsb = sb("epsb", [1, 1], F32)
        nmx = sb("nmx", [128, 1], F32)
        ssum = sb("ssum", [128, 1], F32)
        srcp4 = sb("srcp4", [128, 4], F32)

        psA = psum("psA", [128, 1024], F32)
        psB = psum("psB", [128, 1024], F32)
        psC = psum("psC", [128, 512], F32)
        psD = psum("psD", [128, 512], F32)
        psE = psum("psE", [128, 1024], BF16)
        psF = psum("psF", [128, 512], F32)
        psAB = (psA, psB)

        sems = {n: st.enter_context(nc.semaphore(n))
                for n in ("dW", "dS", "dO", "PE", "ACT", "DVE", "CC")}

        xld = (cosb[:, 0:512], cosb[:, 512:1024])
        tsb = (sinb[:, 0:512], sinb[:, 512:1024])
        gld = (q0[:, 0:512], q0[:, 512:1024])

        def mm(out, l, r, start, stop, waits=(), inc=None):
            return ir.emit("pe", (lambda e, o=out, ll=l, rr=r, s1=start, s2=stop:
                                  e.matmul(o, ll, rr, start=s1, stop=s2)), waits, inc)

        def dma(eng, out, in_, waits=(), sem="dS"):
            return ir.emit(eng, (lambda e, o=out, i=in_: e.dma_start(out=o, in_=i)),
                           waits, inc=(sem, 16))

        xq_tok = []
        xn3 = xn.ap().rearrange("p (c t) -> p c t", c=32)
        for i in range(4):
            src = xT.ap()[i * 1024:(i + 1) * 1024, :].rearrange("(c p) t -> p c t", p=128)
            xq_tok.append(dma("gp", xn3[:, i * 8:(i + 1) * 8, :], src, (), "dW"))
        t_art = dma("gp", artb.ap().rearrange("p (c m) -> p c m", c=32),
                    art.ap().rearrange("(c p) m -> p c m", p=128), (), "dW")
        t_cores = dma("gp", coresb[:, :], coresst[:, :], (), "dW")
        t_bwt = dma("gp", ctx[0:16, 0:4096], bwt[:, :], (), "dW")
        t_cos = dma("sp", cosb[:, :], cosT[:, :])
        t_sin = dma("sp", sinb[:, :], sinST[:, :])
        t_tri = dma("sp", trib[:, :], tri[:, :])
        t_idb = dma("sp", idb[:, :], identb[:, :])
        t_idf = dma("sp", idf[:, :], identf[:, :])
        t_bb = dma("sp", bbb[:, :], bb[:, :])
        repb = sb("repb", [16, 128], BF16)
        selb = sb("selb", [8, 128], BF16)
        t_rep = dma("sp", repb[:, :], rep16[:, :])
        t_sel = dma("sp", selb[:, :], sel8[:, :])

        def ms(ap, v, waits=(), inc=None):
            return ir.emit("dve", (lambda e, a=ap, vv=v: e.memset(a, vv)), waits, inc)

        ms(onesb[:, :], 1.0)
        ms(onesf[:, :], 1.0)
        ms(b16[:, :], -16.0)
        ms(epsb[:, :], 1e-5)
        t_consts = ir.emit("dve", lambda e: e.nop(), (), inc=("DVE", 1))

        free = {"psC": None, "psD": None, "psE": None, "psF": None,
                "psA": None, "psB": None, "wb0": None, "wb1": None,
                "q0": None, "q1": None, "p0": None, "p1": None, "kvar": None}

        tok_norm = [None, None]
        sq_pair = (sq0, sq1)
        sq_free = [None, None]
        tok_bc_prev = None

        for t2 in (0, 1):
            tok_sq = [None] * 32
            tok_mean_last = None
            tok_sq_last = None
            for c in range(32):
                xtok = xq_tok[c // 8]
                xs = xn[:, c * 1024 + t2 * 512: c * 1024 + t2 * 512 + 512]
                tok_sq[c] = ir.emit(
                    "act", (lambda e, o=sq_pair[c % 2][:, :], i=xs: e.square(o, i)),
                    _wt(xtok, sq_free[c % 2]), inc=("ACT", 1))
                w0 = _wt(xtok, free["psC"], t_consts) if c == 0 else _wt(xtok)
                mm(psC[0:1, :], onesb[:, :], xs, c == 0, c == 31, w0,
                   inc=("PE", 1) if c == 31 else None)
                if c == 31:
                    tok_mean_last = ("PE", ir.val("PE"))
                wsq = _wt(tok_sq[c], free["psD"] if c == 0 else None)
                sq_free[c % 2] = mm(psD[0:1, :], onesb[:, :], sq_pair[c % 2][:, :],
                                    c == 0, c == 31, wsq, inc=("PE", 1))
                if c == 31:
                    tok_sq_last = sq_free[c % 2]
            ir.emit("act", lambda e: e.mul(ra[:, :], psC[0:1, :], 1.0 / 4096.0),
                    _wt(tok_mean_last, tok_bc_prev))
            ir.emit("act", lambda e: e.mul(rb[:, :], psD[0:1, :], 1.0 / 4096.0),
                    _wt(tok_sq_last))
            tok_mu = ir.emit("act", lambda e: e.square(rc[:, :], ra[:, :]), (),
                             inc=("ACT", 1))
            free["psC"] = tok_mu
            free["psD"] = tok_mu
            tok_var = ir.emit("dve", lambda e: e.tensor_tensor(
                rb[:, :], rb[:, :], rc[:, :], ALU.subtract), _wt(tok_mu), inc=("DVE", 1))
            tok_sqrt = ir.emit("act", lambda e: e.activation(
                rc[:, :], rb[:, :], AF.Sqrt, bias=epsb[:, :]),
                _wt(tok_var, t_consts), inc=("ACT", 1))
            tok_r = ir.emit("dve", lambda e: e.reciprocal(rb[:, :], rc[:, :]),
                            _wt(tok_sqrt), inc=("DVE", 1))
            mm(psF[:, :], onesf[:, :], ra[:, :], True, True,
               _wt(tok_mu, free["psF"], t_consts))
            tok_bc = mm(psA[:, 0:512], onesf[:, :], rb[:, :], True, True,
                        _wt(tok_r, free["psA"]), inc=("PE", 1))
            tok_bc_prev = tok_bc
            for c in range(32):
                xs = xn[:, c * 1024 + t2 * 512: c * 1024 + t2 * 512 + 512]
                ir.emit("dve", (lambda e, a=xs: e.tensor_tensor(
                    a, a, psF[:, :], ALU.subtract)), _wt(tok_bc) if c == 0 else ())
                inc = ("DVE", 1) if c == 31 else None
                tk = ir.emit("dve", (lambda e, a=xs: e.tensor_tensor(
                    a, a, psA[:, 0:512], ALU.mult)), (), inc)
            tok_norm[t2] = tk
            free["psF"] = tk
            free["psA"] = tk

        tok_norm_all = tok_norm[1]

        wbuf = (wb0, wb1)
        wbuf3 = tuple(w.ap().rearrange("p (c m) -> p c m", c=32) for w in wbuf)
        n_kag = 0
        for hb in range(16):
            par = hb % 2
            src = wkv.ap()[hb * 4096:(hb + 1) * 4096, :].rearrange("(c p) m -> p c m", p=128)
            t_w = dma("gp", wbuf3[par], src, _wt(free["wb" + str(par)]), "dW")
            ps = psAB[par]
            pname = "psA" if par == 0 else "psB"
            tok_mm = None
            for c in range(32):
                for half in (0, 1):
                    w0 = _wt(t_w, free[pname], tok_norm_all) if (c == 0 and half == 0) else ()
                    inc = ("PE", 1) if (c == 31 and half == 1) else None
                    tok_mm = mm(ps[:, half * 512:(half + 1) * 512],
                                wbuf[par][:, c * 128:(c + 1) * 128],
                                xn[:, c * 1024 + half * 512: c * 1024 + half * 512 + 512],
                                c == 0, c == 31, w0, inc) or tok_mm
            free["wb" + str(par)] = tok_mm
            tok_ev = ir.emit("dve", (lambda e, s=ps: e.tensor_copy(q0[:, :], s[:, :])),
                             _wt(tok_mm, free["q0"]), inc=("DVE", 1))
            free[pname] = tok_ev
            if hb < 8:
                t_r1 = dma("sp", q1[0:64, :], q0[64:128, :], _wt(tok_ev, free["q1"]))
                t_r2 = dma("sp", q1[64:128, :], q0[0:64, :], ())
                ir.emit("dve", lambda e: e.tensor_tensor(
                    p0[:, :], q0[:, :], cosb[:, :], ALU.mult), _wt(t_cos, free["p0"]))
                ir.emit("dve", lambda e: e.tensor_tensor(
                    q1[:, :], q1[:, :], sinb[:, :], ALU.mult), _wt(t_r2, t_sin))
                tok_k = ir.emit("dve", lambda e: e.tensor_tensor(
                    q0[:, :], p0[:, :], q1[:, :], ALU.add), (), inc=("DVE", 1))
                free["p0"] = tok_k
                free["q1"] = tok_k
                tq0 = dma("sp", kag_in.ap()[hb * 128:(hb + 1) * 128, :], q0[:, :],
                          _wt(tok_k))
                free["q0"] = tq0
                n_kag += 1
            else:
                hv = hb - 8
                tok_tr = None
                for j in range(8):
                    w0 = _wt(tok_ev, free["psE"], t_idb) if j == 0 else ()
                    inc = ("PE", 1) if j == 7 else None
                    tok_tr = ir.emit("pe", (lambda e, j=j: e.transpose(
                        psE[:, j * 128:(j + 1) * 128], q0[:, j * 128:(j + 1) * 128],
                        idb[:, :])), w0, inc) or tok_tr
                free["q0"] = tok_tr
                tok_ev2 = ir.emit("dve", lambda e: e.tensor_copy(q1[:, :], psE[:, :]),
                                  _wt(tok_tr, free["q1"]), inc=("DVE", 1))
                free["psE"] = tok_ev2
                dst = kag_in.ap()[1024:2048, :].rearrange("(j p) m -> p j m", p=128)[
                    :, :, hv * 128:(hv + 1) * 128]
                tq1 = dma("sp", dst, q1.ap().rearrange("p (j m) -> p j m", j=8),
                          _wt(tok_ev2))
                free["q1"] = tq1
                n_kag += 1

        tok_cc = ir.emit("gp", lambda e: e.collective_compute(
            "AllGather", ALU.bypass,
            replica_groups=[[0, 1, 2, 3], [4, 5, 6, 7]],
            ins=[kag_in.ap().opt()], outs=[kag_out.ap().opt()]),
            [("dS", ir.val("dS"))], inc=("CC", 1))

        ir.emit("gp", lambda e: e.nop(), [("CC", ir.val("CC"))])

        g_free = [None, None]
        for t2 in (0, 1):
            tok_ar = None
            tok_ar2 = None
            for c in range(32):
                xs = xn[:, c * 1024 + t2 * 512: c * 1024 + t2 * 512 + 512]
                w0 = _wt(free["psC"], tok_norm_all, t_art) if c == 0 else ()
                inc = ("PE", 1) if c == 31 else None
                tok_ar = mm(psC[0:16, :], artb[:, c * 40:c * 40 + 16],
                            xs, c == 0, c == 31, w0, inc) or tok_ar
                w0b = _wt(free["psD"]) if c == 0 else ()
                tok_ar2 = mm(psD[0:8, :], artb[:, c * 40 + 32:c * 40 + 40],
                             xs, c == 0, c == 31, w0b, inc) or tok_ar2
            tok_ax = ir.emit("act", (lambda e, t2=t2: e.mul(
                axb[:, t2 * 512:(t2 + 1) * 512], psC[0:16, :], 2.0)),
                _wt(tok_ar), inc=("ACT", 1))
            tok_rl = ir.emit("act", lambda e: e.copy(rlb.ap(), psD[0:8, :]),
                             _wt(tok_ar2), inc=("ACT", 1))
            free["psC"] = tok_ax
            free["psD"] = tok_rl
            tok_pf = None
            tok_ex = None
            for j in range(4):
                wrt = _wt(tok_rl, free["psE"]) if j == 0 else _wt(tok_ex)
                tok_rt = ir.emit("pe", (lambda e, j=j: e.transpose(
                    psE[:, 512 + j * 8: 512 + (j + 1) * 8],
                    rlb[:, j * 128:(j + 1) * 128], idb[0:8, 0:8])),
                    wrt, inc=("PE", 1))
                ir.emit("dve", (lambda e, j=j: e.reduce_max(
                    nmx[:, :], psE[:, 512 + j * 8: 512 + (j + 1) * 8], AX.X,
                    negate=True)), _wt(tok_rt))
                tok_nm = ir.emit("dve", lambda e: e.nop(), (), inc=("DVE", 1))
                tok_ex = ir.emit("act", (lambda e, j=j: e.activation(
                    ptm[:, j * 8:(j + 1) * 8], psE[:, 512 + j * 8: 512 + (j + 1) * 8],
                    AF.Exp, bias=nmx[:, :], accum_out=ssum[:, :])),
                    _wt(tok_nm), inc=("ACT", 1))
                tok_rc = ir.emit("dve", (lambda e, j=j: e.reciprocal(
                    srcp4[:, j:j + 1], ssum[:, :])), _wt(tok_ex), inc=("DVE", 1))
                tok_pr = ir.emit("act", (lambda e, j=j: e.activation(
                    ptm[:, j * 8:(j + 1) * 8], ptm[:, j * 8:(j + 1) * 8],
                    AF.Identity, scale=srcp4[:, j:j + 1])), _wt(tok_rc),
                    inc=("ACT", 1))
                tok_tp = ir.emit("pe", (lambda e, j=j: e.transpose(
                    psE[0:8, j * 128:(j + 1) * 128], ptm[:, j * 8:(j + 1) * 8],
                    idb[:, :])), _wt(tok_pr), inc=("PE", 1))
                if j == 3:
                    tok_pf = tok_tp
            tok_pfc = ir.emit("dve", lambda e: e.tensor_copy(pfb[:, :], psE[0:8, 0:512]),
                              _wt(tok_pf), inc=("DVE", 1))
            free["psE"] = tok_pfc
            tok_axr = mm(psF[:, :], repb[:, :], axb[0:16, t2 * 512:(t2 + 1) * 512],
                         True, True, _wt(tok_ax, free["psF"], t_rep), inc=("PE", 1))
            tok_prr = mm(psA[:, 0:512], selb[:, :], pfb[:, :], True, True,
                         _wt(tok_pfc, free["psA"], t_sel), inc=("PE", 1))
            tok_prc = ir.emit("dve", lambda e: e.tensor_copy(prb[:, :], psA[:, 0:512]),
                              _wt(tok_prr), inc=("DVE", 1))
            free["psA"] = tok_prc
            tok_axp = ir.emit("dve", lambda e: e.tensor_tensor(
                axpb[:, :], psF[:, :], prb[:, :], ALU.mult), _wt(tok_axr),
                inc=("DVE", 1))
            free["psF"] = tok_axp
            tok_mixmm = mm(psC[0:16, :], coresb[:, :], axpb[:, :], True, True,
                           _wt(tok_axp, free["psC"], t_cores), inc=("PE", 1))
            tok_mix = ir.emit("dve", lambda e: e.tensor_copy(mixb[:, :], psC[0:16, :]),
                              _wt(tok_mixmm), inc=("DVE", 1))
            free["psC"] = tok_mix
            for m in range(32):
                par = m % 2
                src = wb.ap()[m * 4096:(m + 1) * 4096, :].rearrange(
                    "(c p) m -> p c m", p=128)
                t_w = dma("gp", wbuf3[par], src, _wt(free["wb" + str(par)]), "dW")
                ps = psAB[par]
                pname = "psA" if par == 0 else "psB"
                tok_mm = None
                for c in range(32):
                    w0 = _wt(t_w, free[pname], tok_mix, t_bwt) if c == 0 else ()
                    tok_mm = mm(ps[:, 0:512], wbuf[par][:, c * 128:(c + 1) * 128],
                                xn[:, c * 1024 + t2 * 512: c * 1024 + t2 * 512 + 512],
                                c == 0, False, w0) or tok_mm
                free["wb" + str(par)] = None
                tok_mm = mm(ps[:, 0:512], ctx[0:16, m * 128:(m + 1) * 128],
                            mixb[:, :], False, True, (), inc=("PE", 1))
                free["wb" + str(par)] = tok_mm
                tok_g = ir.emit("act", (lambda e, par=par, m=m, s=ps: e.activation(
                    sq_pair[par][:, :], s[:, 0:512], AF.Silu,
                    bias=bbb[:, m:m + 1])), _wt(tok_mm, g_free[par], t_bb),
                    inc=("ACT", 1))
                free[pname] = tok_g
                g_free[par] = dma("sp", gbuf.ap()[m * 128:(m + 1) * 128,
                                                  t2 * 512:(t2 + 1) * 512],
                                  sq_pair[par][:, :], _wt(tok_g))

        kvk = kvar.ap()[:, 0:4096].rearrange("p (j q) -> p j q", j=8)
        kvv = kvar.ap()[:, 4096:8192].rearrange("p (j q) -> p j q", j=8)
        ppair = (p0, p1)
        tok_ctxn = None
        tok_attn_prev = None
        free["q0"] = free["q0"]
        for hk in range(8):
            t_kv = None
            for c2 in range(4):
                srcK = kag_out.ap()[c2 * 2048 + hk * 128: c2 * 2048 + (hk + 1) * 128,
                                    :].rearrange("p (j m) -> p j m", j=8)
                w0 = _wt(tok_cc, free["kvar"]) if c2 == 0 else ()
                dma("sp", kvk[:, :, c2 * 128:(c2 + 1) * 128], srcK, w0)
                srcV = kag_out.ap()[c2 * 2048 + 1024: c2 * 2048 + 2048,
                                    hk * 128:(hk + 1) * 128].rearrange(
                    "(j p) m -> p j m", p=128)
                t_kv = dma("sp", kvv[:, :, c2 * 128:(c2 + 1) * 128], srcV, ())
            for g4 in range(4):
                h = hk * 4 + g4
                par = h % 2
                src = wq.ap()[h * 4096:(h + 1) * 4096, :].rearrange(
                    "(c p) m -> p c m", p=128)
                t_w = dma("gp", wbuf3[par], src, _wt(free["wb" + str(par)]), "dW")
                tok_qp = None
                for half in (0, 1):
                    for c in range(32):
                        w0 = _wt(t_w, free["psF"]) if c == 0 else ()
                        inc = ("PE", 1) if c == 31 else None
                        tok_qp = mm(psF[:, :], wbuf[par][:, c * 128:(c + 1) * 128],
                                    xn[:, c * 1024 + half * 512:
                                       c * 1024 + half * 512 + 512],
                                    c == 0, c == 31, w0, inc) or tok_qp
                    tk = ir.emit("dve", (lambda e, half=half: e.tensor_copy(
                        q0[:, half * 512:(half + 1) * 512], psF[:, :])),
                        _wt(tok_qp, free["q0"] if half == 0 else None,
                            tok_attn_prev if half == 0 else None),
                        inc=("DVE", 1))
                    free["psF"] = tk
                free["wb" + str(par)] = tok_qp
                tok_qe = tk
                t_r1 = dma("sp", q1[0:64, :], q0[64:128, :], _wt(tok_qe, free["q1"]))
                t_r2 = dma("sp", q1[64:128, :], q0[0:64, :], ())
                ir.emit("dve", lambda e: e.tensor_tensor(
                    p0[:, :], q0[:, :], cosb[:, :], ALU.mult), _wt(free["p0"]))
                ir.emit("dve", lambda e: e.tensor_tensor(
                    q1[:, :], q1[:, :], sinb[:, :], ALU.mult), _wt(t_r2))
                tok_rope = ir.emit("dve", lambda e: e.tensor_tensor(
                    q0[:, :], p0[:, :], q1[:, :], ALU.add), (), inc=("DVE", 1))
                free["q1"] = tok_rope
                free["p0"] = tok_rope
                for J in (0, 1):
                    groups = [(jp, p) for jp in range(4 * (J + 1)) for p in (0, 1)]
                    ng = len(groups)
                    smm = [None] * ng
                    expd = [None] * ng
                    avl = [None] * ng
                    maskd = [None] * ng

                    def emit_avl(gi, last):
                        jp, p = groups[gi]
                        off = (jp - 4 * J) * 128 if jp >= 4 * J else 0
                        spar = gi % 2
                        P = ppair[spar]
                        w0 = _wt(maskd[gi] or expd[gi],
                                 tok_ctxn if gi == 0 else None)
                        for b in (0, 1):
                            c2 = 2 * p + b
                            mm(psC[:, off:512],
                               kvar[:, 4096 + jp * 512 + c2 * 128:
                                    4096 + jp * 512 + c2 * 128 + 128],
                               P[:, b * 512 + off:(b + 1) * 512],
                               gi == 0 and b == 0, last and b == 1,
                               w0 if b == 0 else ())
                        for b in (0, 1):
                            inc = ("PE", 1) if b == 1 else None
                            avl[gi] = mm(psD[0:1, off:512], onesb[:, :],
                                         ppair[spar][:, b * 512 + off:(b + 1) * 512],
                                         gi == 0 and b == 0, last and b == 1,
                                         (), inc) or avl[gi]

                    for gi, (jp, p) in enumerate(groups):
                        off = (jp - 4 * J) * 128 if jp >= 4 * J else 0
                        spar = gi % 2
                        ps = psAB[spar]
                        P = ppair[spar]
                        for b in (0, 1):
                            c2 = 2 * p + b
                            w0 = ()
                            if b == 0:
                                w0 = _wt(expd[gi - 2] if gi >= 2 else
                                         free["psA" if spar == 0 else "psB"],
                                         tok_rope if gi == 0 else None,
                                         t_kv if (gi == 0 and g4 == 0 and J == 0)
                                         else None)
                            inc = ("PE", 1) if b == 1 else None
                            smm[gi] = mm(ps[:, b * 512 + off:(b + 1) * 512],
                                         kvar[:, jp * 512 + c2 * 128:
                                              jp * 512 + c2 * 128 + 128],
                                         q0[:, J * 512 + off:(J + 1) * 512],
                                         True, True, w0, inc) or smm[gi]
                        ps3 = ps.ap().rearrange("p (b m) -> p b m", b=2)
                        P3 = P.ap().rearrange("p (b m) -> p b m", b=2)
                        expd[gi] = ir.emit("act", (lambda e, i=ps3[:, :, off:512],
                                                   o=P3[:, :, off:512]:
                                                   e.activation(o, i, AF.Exp,
                                                                bias=b16[:, :],
                                                                scale=SCALE)),
                                           _wt(smm[gi],
                                               avl[gi - 2] if gi >= 2 else
                                               free["p0" if spar == 0 else "p1"],
                                               t_consts if gi == 0 else None),
                                           inc=("ACT", 1))
                        if jp >= 4 * J:
                            if off > 0:
                                ir.emit("dve", (lambda e, a=P3[:, :, 0:off]:
                                                e.memset(a, 0.0)), _wt(expd[gi]))
                                wmask = ()
                            else:
                                wmask = _wt(expd[gi])
                            tri3 = trib.ap()[:, 2 * p * 128:(2 * p + 2) * 128].rearrange(
                                "p (b m) -> p b m", b=2)
                            maskd[gi] = ir.emit("dve", (lambda e,
                                                        a=P3[:, :, off:off + 128],
                                                        t=tri3:
                                                        e.tensor_tensor(a, a, t,
                                                                        ALU.mult)),
                                                _wt(expd[gi], t_tri),
                                                inc=("DVE", 1))
                        if gi >= 1:
                            emit_avl(gi - 1, False)
                    emit_avl(ng - 1, True)
                    tok_attn_prev = smm[ng - 1]
                    free["psB"] = expd[ng - 1]
                    free["psA"] = expd[ng - 2]
                    free["p0"] = avl[ng - 1]
                    free["p1"] = avl[ng - 1]
                    if J == 1:
                        free["kvar"] = avl[ng - 1]
                    tok_l = ir.emit("dve", lambda e: e.tensor_copy(
                        ra[:, :], psD[0:1, :]), _wt(avl[ng - 1]))
                    tok_rcp = ir.emit("dve", lambda e: e.reciprocal(
                        rb[:, :], ra[:, :]), (), inc=("DVE", 1))
                    tok_lb = mm(psD[:, :], onesf[:, :], rb[:, :], True, True,
                                _wt(tok_rcp), inc=("PE", 1))
                    ir.emit("dve", lambda e: e.tensor_copy(lbb[:, :], psD[:, :]),
                            _wt(tok_lb))
                    tok_ctxn = ir.emit("dve", (lambda e, h=h, J=J:
                                               e.tensor_tensor(
                                                   ctx[:, h * 1024 + J * 512:
                                                       h * 1024 + J * 512 + 512],
                                                   psC[:, :], lbb[:, :], ALU.mult)),
                                       (), inc=("DVE", 1))
        tok_ctx_all = tok_ctxn

        comb = [None, None]
        outd = [None, None]
        for m in range(32):
            par = m % 2
            src = wo.ap()[m * 4096:(m + 1) * 4096, :].rearrange("(c p) m -> p c m",
                                                                p=128)
            t_w = dma("gp", wbuf3[par], src, _wt(free["wb" + str(par)]), "dW")
            tok_s = None
            for half in (0, 1):
                u = 2 * m + half
                bp = u % 2
                ps = psAB[bp]
                pname = "psA" if bp == 0 else "psB"
                for c in range(32):
                    w0 = _wt(t_w, free[pname], tok_ctx_all) if c == 0 else ()
                    inc = ("PE", 1) if c == 31 else None
                    tok_s = mm(ps[:, 0:512], wbuf[par][:, c * 128:(c + 1) * 128],
                               ctx[:, c * 1024 + half * 512:
                                   c * 1024 + half * 512 + 512],
                               c == 0, c == 31, w0, inc) or tok_s
                first = _wt(g_free[0], g_free[1], tok_ctx_all,
                            tok_attn_prev) if u <= 1 else []
                t_g = dma("sp", gld[bp], gbuf.ap()[m * 128:(m + 1) * 128,
                                                   half * 512:(half + 1) * 512],
                          _wt(comb[bp]) + first)
                t_x = dma("sp", xld[bp], xT.ap()[m * 128:(m + 1) * 128,
                                                 half * 512:(half + 1) * 512], ())
                ir.emit("dve", (lambda e, s=ps, b=bp: e.tensor_tensor(
                    tsb[b], s[:, 0:512], gld[b], ALU.mult)),
                    _wt(tok_s, t_g, t_x, outd[bp]))
                comb[bp] = ir.emit("dve", (lambda e, b=bp: e.tensor_tensor(
                    tsb[b], tsb[b], xld[b], ALU.add)), (), inc=("DVE", 1))
                free[pname] = comb[bp]
                outd[bp] = ir.emit("sp", (lambda e, b=bp, m=m, half=half:
                                          e.dma_start(out=outT.ap()[
                                              m * 128:(m + 1) * 128,
                                              half * 512:(half + 1) * 512],
                                              in_=tsb[b])),
                                   _wt(comb[bp]), inc=("dO", 16))
            free["wb" + str(par)] = tok_s

        ir.emit("sp", lambda e: e.nop(), [("dO", ir.val("dO"))])
        ir.emit("gp", lambda e: e.nop(), [("dO", ir.val("dO"))])

        with nc.Block() as blk:
            def run(e, eng):
                for ws, fn, inc in ir.items[eng]:
                    for s, v in ws:
                        e.wait_ge(sems[s], v)
                    inst = fn(e)
                    if inc is not None:
                        inst.then_inc(sems[inc[0]], inc[1])

            @blk.tensor
            def _(e):
                run(e, "pe")

            @blk.scalar
            def _(e):
                run(e, "act")

            @blk.vector
            def _(e):
                run(e, "dve")

            @blk.sync
            def _(e):
                run(e, "sp")

            @blk.gpsimd
            def _(e):
                run(e, "gp")

    return nc


def _rope_tables():
    import jax
    with jax.default_device(jax.devices("cpu")[0]):
        import jax.numpy as jnp
        DH = 128
        ROPE_THETA = 500000.0
        ROPE_FACTOR = 32.0
        HI_FREQ_FACTOR = 4.0
        LO_FREQ_FACTOR = 1.0
        ORIG_CTX = 8192
        inv_freq = 1.0 / ROPE_THETA ** (jnp.arange(0, DH, 2, dtype=jnp.float32) / DH)
        wavelen = 2.0 * math.pi / inv_freq
        cut_lo = ORIG_CTX / HI_FREQ_FACTOR
        cut_hi = ORIG_CTX / LO_FREQ_FACTOR
        inv_freq = jnp.where(wavelen > cut_hi, inv_freq / ROPE_FACTOR, inv_freq)
        smooth = (ORIG_CTX / wavelen - LO_FREQ_FACTOR) / (HI_FREQ_FACTOR - LO_FREQ_FACTOR)
        smoothed = (1.0 - smooth) * inv_freq / ROPE_FACTOR + smooth * inv_freq
        is_medium = ~(wavelen < cut_hi) & ~(wavelen > cut_lo)
        inv_freq = jnp.where(is_medium, smoothed, inv_freq)
        pos = jnp.arange(4096, dtype=jnp.float32)
        freqs = pos[:, None] * inv_freq[None, :]
        emb = jnp.concatenate([freqs, freqs], axis=-1)
        cos = np.asarray(jnp.cos(emb), dtype=np.float32)
        sin = np.asarray(jnp.sin(emb), dtype=np.float32)
    return cos, sin


def kernel(L0_attn_x, L0_attn_Wq, L0_attn_Wk, L0_attn_Wv, L0_attn_Wo,
           L0_ffn_W_base, L0_ffn_b_base, L0_ffn_A_w, L0_ffn_B_w,
           L0_ffn_router_w, L0_ffn_cores):
    x = np.ascontiguousarray(np.asarray(L0_attn_x, dtype=np.float32))
    f32 = lambda a: np.ascontiguousarray(np.asarray(a, dtype=np.float32))
    bfc = lambda a: np.ascontiguousarray(np.asarray(a).astype(BF))

    def blocks(w):
        w = f32(w)
        nb = w.shape[0] // 128
        return np.ascontiguousarray(
            w.reshape(nb, 128, 4096).transpose(0, 2, 1).reshape(nb * 4096, 128))

    wq = blocks(L0_attn_Wq)
    wkv = blocks(np.concatenate([f32(L0_attn_Wk), f32(L0_attn_Wv)], axis=0))
    wo = blocks(L0_attn_Wo)
    wb = blocks(L0_ffn_W_base)
    art = np.ascontiguousarray(
        np.concatenate([f32(L0_ffn_A_w), np.zeros((16, 4096), np.float32),
                        f32(L0_ffn_router_w)], axis=0).T)
    bwt = np.ascontiguousarray(f32(L0_ffn_B_w).T)
    coresst = np.ascontiguousarray(f32(L0_ffn_cores).reshape(128, 16))
    bb = np.ascontiguousarray(f32(L0_ffn_b_base).reshape(32, 128).T)
    identb = np.eye(128, dtype=BF)
    identf = np.eye(128, dtype=np.float32)
    rep16 = np.zeros((16, 128), dtype=BF)
    for r in range(16):
        for e_ in range(8):
            rep16[r, e_ * 16 + r] = 1
    sel8 = np.zeros((8, 128), dtype=BF)
    for e_ in range(8):
        sel8[e_, e_ * 16:(e_ + 1) * 16] = 1

    cos, sin = _rope_tables()
    sinS = sin.T.copy()
    sinS[0:64, :] *= -1.0
    cosTfull = np.ascontiguousarray(cos.T)

    if "nc" not in _NC_CACHE:
        _NC_CACHE["nc"] = build_nc()
    nc = _NC_CACHE["nc"]

    in_maps = []
    owns = []
    for core in range(8):
        b, c = divmod(core, 4)
        own = np.arange(1024) * 4 + c
        owns.append((b, own))
        tri = np.zeros((128, 512), dtype=BF)
        for c2 in range(4):
            tri[:, c2 * 128:(c2 + 1) * 128] = np.triu(
                np.ones((128, 128), dtype=np.float32), 0 if c2 <= c else 1
            ).astype(BF)
        in_maps.append({
            "xT": np.ascontiguousarray(x[b].T[:, own]),
            "wq": wq, "wkv": wkv, "wo": wo, "wb": wb,
            "art": art, "bwt": bwt, "coresst": coresst, "bb": bb,
            "cosT": np.ascontiguousarray(cosTfull[:, own]),
            "sinST": np.ascontiguousarray(sinS[:, own]),
            "tri": tri, "identb": identb, "identf": identf,
            "rep16": rep16, "sel8": sel8,
        })

    res = run_bass_kernel_spmd(nc, in_maps, core_ids=list(range(8)))

    out = np.empty((2, 4096, 4096), dtype=np.float32)
    for core in range(8):
        b, own = owns[core]
        out[b, own, :] = res.results[core]["outT"].T
    return out


# version 09
# speedup vs baseline: 1.0103x; 1.0103x over previous
import math
from contextlib import ExitStack

import numpy as np
import ml_dtypes
import concourse.bass as bass
import concourse.mybir as mybir
from concourse.bass_utils import run_bass_kernel_spmd

F32 = mybir.dt.float32
BF16 = mybir.dt.bfloat16
AF = mybir.ActivationFunctionType
ALU = mybir.AluOpType
AX = mybir.AxisListType
BF = ml_dtypes.bfloat16

SCALE = 1.0 / math.sqrt(128.0)
ENGS = ("pe", "act", "dve", "sp", "gp")

_NC_CACHE = {}


class IR:

    def __init__(self):
        self.items = {e: [] for e in ENGS}
        self.cnt = {}

    def val(self, sem):
        return self.cnt.get(sem, 0)

    def emit(self, eng, fn, waits=(), inc=None):
        ws = tuple((s, v) for (s, v) in waits if v is not None and v > 0)
        self.items[eng].append((ws, fn, inc))
        if inc is not None:
            sem, k = inc
            self.cnt[sem] = self.cnt.get(sem, 0) + k
            return (sem, self.cnt[sem])
        return None


def _wt(*tokens):
    best = {}
    for t in tokens:
        if t is None:
            continue
        s, v = t
        if v is not None and (s not in best or v > best[s]):
            best[s] = v
    return [(s, v) for s, v in best.items()]


def build_nc():
    nc = bass.Bass("TRN2", target_bir_lowering=False)

    xT = nc.dram_tensor("xT", [4096, 1024], F32, kind="ExternalInput")
    wq = nc.dram_tensor("wq", [32 * 4096, 128], BF16, kind="ExternalInput")
    wkv = nc.dram_tensor("wkv", [16 * 4096, 128], BF16, kind="ExternalInput")
    wo = nc.dram_tensor("wo", [32 * 4096, 128], BF16, kind="ExternalInput")
    wb = nc.dram_tensor("wb", [32 * 4096, 128], BF16, kind="ExternalInput")
    art = nc.dram_tensor("art", [4096, 40], F32, kind="ExternalInput")
    bwt = nc.dram_tensor("bwt", [16, 4096], F32, kind="ExternalInput")
    coresst = nc.dram_tensor("coresst", [128, 16], F32, kind="ExternalInput")
    bb = nc.dram_tensor("bb", [128, 32], F32, kind="ExternalInput")
    cosT = nc.dram_tensor("cosT", [128, 1024], F32, kind="ExternalInput")
    sinST = nc.dram_tensor("sinST", [128, 1024], F32, kind="ExternalInput")
    tri = nc.dram_tensor("tri", [128, 512], BF16, kind="ExternalInput")
    identb = nc.dram_tensor("identb", [128, 128], BF16, kind="ExternalInput")
    identf = nc.dram_tensor("identf", [128, 128], F32, kind="ExternalInput")
    rep16 = nc.dram_tensor("rep16", [16, 128], BF16, kind="ExternalInput")
    sel8 = nc.dram_tensor("sel8", [8, 128], BF16, kind="ExternalInput")
    outT = nc.dram_tensor("outT", [4096, 1024], F32, kind="ExternalOutput")
    kag_in = nc.dram_tensor("kag_in", [2048, 1024], BF16)
    kag_out = nc.dram_tensor("kag_out", [8192, 1024], BF16)
    gbuf = nc.dram_tensor("gbuf", [4096, 1024], BF16)

    ir = IR()

    with ExitStack() as st:
        def sb(name, shape, dt):
            return st.enter_context(nc.sbuf_tensor(name, shape, dt))

        def psum(name, shape, dt):
            return st.enter_context(nc.psum_tensor(name, shape, dt))

        xn = sb("xn", [128, 32 * 1024], BF16)
        ctx = sb("ctx", [128, 32 * 1024], BF16)
        kvar = sb("kvar", [128, 8192], BF16)
        wb0 = sb("wb0", [128, 4096], BF16)
        wb1 = sb("wb1", [128, 4096], BF16)
        q0 = sb("q0", [128, 1024], BF16)
        q1 = sb("q1", [128, 1024], BF16)
        p0 = sb("p0", [128, 1024], BF16)
        p1 = sb("p1", [128, 1024], BF16)
        cosb = sb("cosb", [128, 1024], F32)
        sinb = sb("sinb", [128, 1024], F32)
        trib = sb("trib", [128, 512], BF16)
        idb = sb("idb", [128, 128], BF16)
        idf = sb("idf", [128, 128], F32)
        artb = sb("artb", [128, 32 * 40], BF16)
        coresb = sb("coresb", [128, 16], BF16)
        bbb = sb("bbb", [128, 32], F32)
        ra = sb("ra", [1, 512], F32)
        rb = sb("rb", [1, 512], F32)
        rc = sb("rc", [1, 512], F32)
        lall = sb("lall", [64, 512], F32)
        linv = sb("linv", [64, 512], BF16)
        rsb0 = sb("rsb0", [1, 512], BF16)
        rsb1 = sb("rsb1", [1, 512], BF16)
        axb = sb("axb", [16, 1024], BF16)
        rlb = sb("rlb", [8, 512], BF16)
        ptm = sb("ptm", [128, 32], BF16)
        pfb = sb("pfb", [8, 512], BF16)
        prb = sb("prb", [128, 512], BF16)
        axpb = sb("axpb", [128, 512], BF16)
        mixb = sb("mixb", [16, 512], BF16)
        sq0 = sb("sq0", [128, 512], BF16)
        sq1 = sb("sq1", [128, 512], BF16)
        onesb = sb("onesb", [128, 1], BF16)
        onesf = sb("onesf", [1, 128], F32)
        onesbb = sb("onesbb", [1, 128], BF16)
        b16 = sb("b16", [128, 1], F32)
        epsb = sb("epsb", [1, 1], F32)
        nmx = sb("nmx", [128, 1], F32)
        ssum = sb("ssum", [128, 1], F32)
        srcp4 = sb("srcp4", [128, 4], F32)

        psA = psum("psA", [128, 1024], F32)
        psB = psum("psB", [128, 1024], F32)
        psC = psum("psC", [128, 512], F32)
        psD = psum("psD", [128, 512], F32)
        psE = psum("psE", [128, 1024], BF16)
        psF = psum("psF", [128, 512], F32)
        psAB = (psA, psB)

        sems = {n: st.enter_context(nc.semaphore(n))
                for n in ("dW", "dS", "dO", "PE", "ACT", "DVE", "CC")}

        xld = (cosb[:, 0:512], cosb[:, 512:1024])
        tsb = (sinb[:, 0:512], sinb[:, 512:1024])
        gld = (q0[:, 0:512], q0[:, 512:1024])

        def mm(out, l, r, start, stop, waits=(), inc=None):
            return ir.emit("pe", (lambda e, o=out, ll=l, rr=r, s1=start, s2=stop:
                                  e.matmul(o, ll, rr, start=s1, stop=s2)), waits, inc)

        def dma(eng, out, in_, waits=(), sem="dS"):
            return ir.emit(eng, (lambda e, o=out, i=in_: e.dma_start(out=o, in_=i)),
                           waits, inc=(sem, 16))

        xq_tok = []
        xn3 = xn.ap().rearrange("p (c t) -> p c t", c=32)
        for i in range(4):
            src = xT.ap()[i * 1024:(i + 1) * 1024, :].rearrange("(c p) t -> p c t", p=128)
            xq_tok.append(dma("gp", xn3[:, i * 8:(i + 1) * 8, :], src, (), "dW"))
        t_art = dma("gp", artb.ap().rearrange("p (c m) -> p c m", c=32),
                    art.ap().rearrange("(c p) m -> p c m", p=128), (), "dW")
        t_cores = dma("gp", coresb[:, :], coresst[:, :], (), "dW")
        t_bwt = dma("gp", ctx[0:16, 0:4096], bwt[:, :], (), "dW")
        t_cos = dma("sp", cosb[:, :], cosT[:, :])
        t_sin = dma("sp", sinb[:, :], sinST[:, :])
        t_tri = dma("sp", trib[:, :], tri[:, :])
        t_idb = dma("sp", idb[:, :], identb[:, :])
        t_idf = dma("sp", idf[:, :], identf[:, :])
        t_bb = dma("sp", bbb[:, :], bb[:, :])
        repb = sb("repb", [16, 128], BF16)
        selb = sb("selb", [8, 128], BF16)
        t_rep = dma("sp", repb[:, :], rep16[:, :])
        t_sel = dma("sp", selb[:, :], sel8[:, :])

        def ms(ap, v, waits=(), inc=None):
            return ir.emit("dve", (lambda e, a=ap, vv=v: e.memset(a, vv)), waits, inc)

        ms(onesb[:, :], 1.0)
        ms(onesf[:, :], 1.0)
        ms(onesbb[:, :], 1.0)
        ms(b16[:, :], -16.0)
        ms(epsb[:, :], 1e-5)
        t_consts = ir.emit("dve", lambda e: e.nop(), (), inc=("DVE", 1))

        free = {"psC": None, "psD": None, "psE": None, "psF": None,
                "psA": None, "psB": None, "wb0": None, "wb1": None,
                "q0": None, "q1": None, "p0": None, "p1": None, "kvar": None}

        tok_norm = [None, None]
        sq_pair = (sq0, sq1)
        sq_free = [None, None]
        tok_bc_prev = None

        for t2 in (0, 1):
            tok_sq = [None] * 32
            tok_mean_last = None
            tok_sq_last = None
            for c in range(32):
                xtok = xq_tok[c // 8]
                xs = xn[:, c * 1024 + t2 * 512: c * 1024 + t2 * 512 + 512]
                tok_sq[c] = ir.emit(
                    "act", (lambda e, o=sq_pair[c % 2][:, :], i=xs: e.square(o, i)),
                    _wt(xtok, sq_free[c % 2]), inc=("ACT", 1))
                w0 = _wt(xtok, free["psC"], t_consts) if c == 0 else _wt(xtok)
                mm(psC[0:1, :], onesb[:, :], xs, c == 0, c == 31, w0,
                   inc=("PE", 1) if c == 31 else None)
                if c == 31:
                    tok_mean_last = ("PE", ir.val("PE"))
                wsq = _wt(tok_sq[c], free["psD"] if c == 0 else None)
                sq_free[c % 2] = mm(psD[0:1, :], onesb[:, :], sq_pair[c % 2][:, :],
                                    c == 0, c == 31, wsq, inc=("PE", 1))
                if c == 31:
                    tok_sq_last = sq_free[c % 2]
            ir.emit("act", lambda e: e.mul(ra[:, :], psC[0:1, :], 1.0 / 4096.0),
                    _wt(tok_mean_last, tok_bc_prev))
            ir.emit("act", lambda e: e.mul(rb[:, :], psD[0:1, :], 1.0 / 4096.0),
                    _wt(tok_sq_last))
            tok_mu = ir.emit("act", lambda e: e.square(rc[:, :], ra[:, :]), (),
                             inc=("ACT", 1))
            free["psC"] = tok_mu
            free["psD"] = tok_mu
            tok_var = ir.emit("dve", lambda e: e.tensor_tensor(
                rb[:, :], rb[:, :], rc[:, :], ALU.subtract), _wt(tok_mu), inc=("DVE", 1))
            tok_sqrt = ir.emit("act", lambda e: e.activation(
                rc[:, :], rb[:, :], AF.Sqrt, bias=epsb[:, :]),
                _wt(tok_var, t_consts), inc=("ACT", 1))
            tok_r = ir.emit("dve", lambda e: e.reciprocal(rb[:, :], rc[:, :]),
                            _wt(tok_sqrt), inc=("DVE", 1))
            mm(psF[:, :], onesf[:, :], ra[:, :], True, True,
               _wt(tok_mu, free["psF"], t_consts))
            tok_bc = mm(psA[:, 0:512], onesf[:, :], rb[:, :], True, True,
                        _wt(tok_r, free["psA"]), inc=("PE", 1))
            tok_bc_prev = tok_bc
            for c in range(32):
                xs = xn[:, c * 1024 + t2 * 512: c * 1024 + t2 * 512 + 512]
                ir.emit("dve", (lambda e, a=xs: e.tensor_tensor(
                    a, a, psF[:, :], ALU.subtract)), _wt(tok_bc) if c == 0 else ())
                inc = ("DVE", 1) if c == 31 else None
                tk = ir.emit("dve", (lambda e, a=xs: e.tensor_tensor(
                    a, a, psA[:, 0:512], ALU.mult)), (), inc)
            tok_norm[t2] = tk
            free["psF"] = tk
            free["psA"] = tk

        tok_norm_all = tok_norm[1]

        wbuf = (wb0, wb1)
        wbuf3 = tuple(w.ap().rearrange("p (c m) -> p c m", c=32) for w in wbuf)
        n_kag = 0
        for hb in range(16):
            par = hb % 2
            src = wkv.ap()[hb * 4096:(hb + 1) * 4096, :].rearrange("(c p) m -> p c m", p=128)
            t_w = dma("gp", wbuf3[par], src, _wt(free["wb" + str(par)]), "dW")
            ps = psAB[par]
            pname = "psA" if par == 0 else "psB"
            tok_mm = None
            for c in range(32):
                for half in (0, 1):
                    w0 = _wt(t_w, free[pname], tok_norm_all) if (c == 0 and half == 0) else ()
                    inc = ("PE", 1) if (c == 31 and half == 1) else None
                    tok_mm = mm(ps[:, half * 512:(half + 1) * 512],
                                wbuf[par][:, c * 128:(c + 1) * 128],
                                xn[:, c * 1024 + half * 512: c * 1024 + half * 512 + 512],
                                c == 0, c == 31, w0, inc) or tok_mm
            free["wb" + str(par)] = tok_mm
            tok_ev = ir.emit("dve", (lambda e, s=ps: e.tensor_copy(q0[:, :], s[:, :])),
                             _wt(tok_mm, free["q0"]), inc=("DVE", 1))
            free[pname] = tok_ev
            if hb < 8:
                t_r1 = dma("sp", q1[0:64, :], q0[64:128, :], _wt(tok_ev, free["q1"]))
                t_r2 = dma("sp", q1[64:128, :], q0[0:64, :], ())
                ir.emit("dve", lambda e: e.tensor_tensor(
                    p0[:, :], q0[:, :], cosb[:, :], ALU.mult), _wt(t_cos, free["p0"]))
                ir.emit("dve", lambda e: e.tensor_tensor(
                    q1[:, :], q1[:, :], sinb[:, :], ALU.mult), _wt(t_r2, t_sin))
                tok_k = ir.emit("dve", lambda e: e.tensor_tensor(
                    q0[:, :], p0[:, :], q1[:, :], ALU.add), (), inc=("DVE", 1))
                free["p0"] = tok_k
                free["q1"] = tok_k
                tq0 = dma("sp", kag_in.ap()[hb * 128:(hb + 1) * 128, :], q0[:, :],
                          _wt(tok_k))
                free["q0"] = tq0
                n_kag += 1
            else:
                hv = hb - 8
                tok_tr = None
                for j in range(8):
                    w0 = _wt(tok_ev, free["psE"], t_idb) if j == 0 else ()
                    inc = ("PE", 1) if j == 7 else None
                    tok_tr = ir.emit("pe", (lambda e, j=j: e.transpose(
                        psE[:, j * 128:(j + 1) * 128], q0[:, j * 128:(j + 1) * 128],
                        idb[:, :])), w0, inc) or tok_tr
                free["q0"] = tok_tr
                tok_ev2 = ir.emit("dve", lambda e: e.tensor_copy(q1[:, :], psE[:, :]),
                                  _wt(tok_tr, free["q1"]), inc=("DVE", 1))
                free["psE"] = tok_ev2
                dst = kag_in.ap()[1024:2048, :].rearrange("(j p) m -> p j m", p=128)[
                    :, :, hv * 128:(hv + 1) * 128]
                tq1 = dma("sp", dst, q1.ap().rearrange("p (j m) -> p j m", j=8),
                          _wt(tok_ev2))
                free["q1"] = tq1
                n_kag += 1

        tok_cc = ir.emit("gp", lambda e: e.collective_compute(
            "AllGather", ALU.bypass,
            replica_groups=[[0, 1, 2, 3], [4, 5, 6, 7]],
            ins=[kag_in.ap().opt()], outs=[kag_out.ap().opt()]),
            [("dS", ir.val("dS"))], inc=("CC", 1))

        ir.emit("gp", lambda e: e.nop(), [("CC", ir.val("CC"))])

        g_free = [None, None]
        for t2 in (0, 1):
            tok_ar = None
            tok_ar2 = None
            for c in range(32):
                xs = xn[:, c * 1024 + t2 * 512: c * 1024 + t2 * 512 + 512]
                w0 = _wt(free["psC"], tok_norm_all, t_art) if c == 0 else ()
                inc = ("PE", 1) if c == 31 else None
                tok_ar = mm(psC[0:16, :], artb[:, c * 40:c * 40 + 16],
                            xs, c == 0, c == 31, w0, inc) or tok_ar
                w0b = _wt(free["psD"]) if c == 0 else ()
                tok_ar2 = mm(psD[0:8, :], artb[:, c * 40 + 32:c * 40 + 40],
                             xs, c == 0, c == 31, w0b, inc) or tok_ar2
            tok_ax = ir.emit("act", (lambda e, t2=t2: e.mul(
                axb[:, t2 * 512:(t2 + 1) * 512], psC[0:16, :], 2.0)),
                _wt(tok_ar), inc=("ACT", 1))
            tok_rl = ir.emit("act", lambda e: e.copy(rlb.ap(), psD[0:8, :]),
                             _wt(tok_ar2), inc=("ACT", 1))
            free["psC"] = tok_ax
            free["psD"] = tok_rl
            tok_pf = None
            tok_ex = None
            for j in range(4):
                wrt = _wt(tok_rl, free["psE"]) if j == 0 else _wt(tok_ex)
                tok_rt = ir.emit("pe", (lambda e, j=j: e.transpose(
                    psE[:, 512 + j * 8: 512 + (j + 1) * 8],
                    rlb[:, j * 128:(j + 1) * 128], idb[0:8, 0:8])),
                    wrt, inc=("PE", 1))
                ir.emit("dve", (lambda e, j=j: e.reduce_max(
                    nmx[:, :], psE[:, 512 + j * 8: 512 + (j + 1) * 8], AX.X,
                    negate=True)), _wt(tok_rt))
                tok_nm = ir.emit("dve", lambda e: e.nop(), (), inc=("DVE", 1))
                tok_ex = ir.emit("act", (lambda e, j=j: e.activation(
                    ptm[:, j * 8:(j + 1) * 8], psE[:, 512 + j * 8: 512 + (j + 1) * 8],
                    AF.Exp, bias=nmx[:, :], accum_out=ssum[:, :])),
                    _wt(tok_nm), inc=("ACT", 1))
                tok_rc = ir.emit("dve", (lambda e, j=j: e.reciprocal(
                    srcp4[:, j:j + 1], ssum[:, :])), _wt(tok_ex), inc=("DVE", 1))
                tok_pr = ir.emit("act", (lambda e, j=j: e.activation(
                    ptm[:, j * 8:(j + 1) * 8], ptm[:, j * 8:(j + 1) * 8],
                    AF.Identity, scale=srcp4[:, j:j + 1])), _wt(tok_rc),
                    inc=("ACT", 1))
                tok_tp = ir.emit("pe", (lambda e, j=j: e.transpose(
                    psE[0:8, j * 128:(j + 1) * 128], ptm[:, j * 8:(j + 1) * 8],
                    idb[:, :])), _wt(tok_pr), inc=("PE", 1))
                if j == 3:
                    tok_pf = tok_tp
            tok_pfc = ir.emit("dve", lambda e: e.tensor_copy(pfb[:, :], psE[0:8, 0:512]),
                              _wt(tok_pf), inc=("DVE", 1))
            free["psE"] = tok_pfc
            tok_axr = mm(psF[:, :], repb[:, :], axb[0:16, t2 * 512:(t2 + 1) * 512],
                         True, True, _wt(tok_ax, free["psF"], t_rep), inc=("PE", 1))
            tok_prr = mm(psA[:, 0:512], selb[:, :], pfb[:, :], True, True,
                         _wt(tok_pfc, free["psA"], t_sel), inc=("PE", 1))
            tok_prc = ir.emit("dve", lambda e: e.tensor_copy(prb[:, :], psA[:, 0:512]),
                              _wt(tok_prr), inc=("DVE", 1))
            free["psA"] = tok_prc
            tok_axp = ir.emit("dve", lambda e: e.tensor_tensor(
                axpb[:, :], psF[:, :], prb[:, :], ALU.mult), _wt(tok_axr),
                inc=("DVE", 1))
            free["psF"] = tok_axp
            tok_mixmm = mm(psC[0:16, :], coresb[:, :], axpb[:, :], True, True,
                           _wt(tok_axp, free["psC"], t_cores), inc=("PE", 1))
            tok_mix = ir.emit("dve", lambda e: e.tensor_copy(mixb[:, :], psC[0:16, :]),
                              _wt(tok_mixmm), inc=("DVE", 1))
            free["psC"] = tok_mix
            for m in range(32):
                par = m % 2
                src = wb.ap()[m * 4096:(m + 1) * 4096, :].rearrange(
                    "(c p) m -> p c m", p=128)
                t_w = dma("sp", wbuf3[par], src, _wt(free["wb" + str(par)]), "dW")
                ps = psAB[par]
                pname = "psA" if par == 0 else "psB"
                tok_mm = None
                for c in range(32):
                    w0 = _wt(t_w, free[pname], tok_mix, t_bwt) if c == 0 else ()
                    tok_mm = mm(ps[:, 0:512], wbuf[par][:, c * 128:(c + 1) * 128],
                                xn[:, c * 1024 + t2 * 512: c * 1024 + t2 * 512 + 512],
                                c == 0, False, w0) or tok_mm
                free["wb" + str(par)] = None
                tok_mm = mm(ps[:, 0:512], ctx[0:16, m * 128:(m + 1) * 128],
                            mixb[:, :], False, True, (), inc=("PE", 1))
                free["wb" + str(par)] = tok_mm
                tok_g = ir.emit("act", (lambda e, par=par, m=m, s=ps: e.activation(
                    sq_pair[par][:, :], s[:, 0:512], AF.Silu,
                    bias=bbb[:, m:m + 1])), _wt(tok_mm, g_free[par], t_bb),
                    inc=("ACT", 1))
                free[pname] = tok_g
                g_free[par] = dma("sp", gbuf.ap()[m * 128:(m + 1) * 128,
                                                  t2 * 512:(t2 + 1) * 512],
                                  sq_pair[par][:, :], _wt(tok_g))

        kvk = kvar.ap()[:, 0:4096].rearrange("p (j q) -> p j q", j=8)
        kvv = kvar.ap()[:, 4096:8192].rearrange("p (j q) -> p j q", j=8)
        ppair = (p0, p1)
        tok_ctxe = None
        rdma_prev = None
        tok_attn_prev = None
        free["q0"] = free["q0"]
        for hk in range(8):
            t_k4 = None
            t_kv = None
            for c2 in range(4):
                srcK = kag_out.ap()[c2 * 2048 + hk * 128: c2 * 2048 + (hk + 1) * 128,
                                    :].rearrange("p (j m) -> p j m", j=8)
                w0 = _wt(tok_cc, free["kvar"]) if c2 == 0 else ()
                t_k4 = dma("sp", kvk[:, :, c2 * 128:(c2 + 1) * 128], srcK, w0)
            for c2 in range(4):
                srcV = kag_out.ap()[c2 * 2048 + 1024: c2 * 2048 + 2048,
                                    hk * 128:(hk + 1) * 128].rearrange(
                    "(j p) m -> p j m", p=128)
                t_kv = dma("sp", kvv[:, :, c2 * 128:(c2 + 1) * 128], srcV, ())
            for g4 in range(4):
                h = hk * 4 + g4
                par = h % 2
                src = wq.ap()[h * 4096:(h + 1) * 4096, :].rearrange(
                    "(c p) m -> p c m", p=128)
                t_w = dma("gp", wbuf3[par], src, _wt(free["wb" + str(par)]), "dW")
                tok_qp = None
                for half in (0, 1):
                    for c in range(32):
                        w0 = _wt(t_w, free["psF"]) if c == 0 else ()
                        inc = ("PE", 1) if c == 31 else None
                        tok_qp = mm(psF[:, :], wbuf[par][:, c * 128:(c + 1) * 128],
                                    xn[:, c * 1024 + half * 512:
                                       c * 1024 + half * 512 + 512],
                                    c == 0, c == 31, w0, inc) or tok_qp
                    tk = ir.emit("dve", (lambda e, half=half: e.tensor_copy(
                        q0[:, half * 512:(half + 1) * 512], psF[:, :])),
                        _wt(tok_qp, free["q0"] if half == 0 else None,
                            tok_attn_prev if half == 0 else None),
                        inc=("DVE", 1))
                    free["psF"] = tk
                free["wb" + str(par)] = tok_qp
                tok_qe = tk
                t_r1 = dma("sp", q1[0:64, :], q0[64:128, :], _wt(tok_qe, free["q1"]))
                t_r2 = dma("sp", q1[64:128, :], q0[0:64, :], ())
                ir.emit("dve", lambda e: e.tensor_tensor(
                    p0[:, :], q0[:, :], cosb[:, :], ALU.mult), _wt(free["p0"]))
                ir.emit("dve", lambda e: e.tensor_tensor(
                    q1[:, :], q1[:, :], sinb[:, :], ALU.mult), _wt(t_r2))
                tok_rope = ir.emit("dve", lambda e: e.tensor_tensor(
                    q0[:, :], p0[:, :], q1[:, :], ALU.add), (), inc=("DVE", 1))
                free["q1"] = tok_rope
                free["p0"] = tok_rope
                for J in (0, 1):
                    groups = [(jp, p) for jp in range(4 * (J + 1)) for p in (0, 1)]
                    ng = len(groups)
                    smm = [None] * ng
                    expd = [None] * ng
                    avl = [None] * ng

                    def emit_avl(gi, last):
                        jp, p = groups[gi]
                        off = (jp - 4 * J) * 128 if jp >= 4 * J else 0
                        spar = gi % 2
                        P = ppair[spar]
                        w0 = _wt(expd[gi],
                                 tok_ctxe if gi == 0 else None,
                                 t_kv if (gi == 0 and g4 == 0 and J == 0)
                                 else None)
                        for b in (0, 1):
                            c2 = 2 * p + b
                            mm(psC[:, off:512],
                               kvar[:, 4096 + jp * 512 + c2 * 128:
                                    4096 + jp * 512 + c2 * 128 + 128],
                               P[:, b * 512 + off:(b + 1) * 512],
                               gi == 0 and b == 0, last and b == 1,
                               w0 if b == 0 else ())
                        for b in (0, 1):
                            inc = ("PE", 1) if b == 1 else None
                            avl[gi] = mm(psD[0:1, off:512], onesb[:, :],
                                         ppair[spar][:, b * 512 + off:(b + 1) * 512],
                                         gi == 0 and b == 0, last and b == 1,
                                         (), inc) or avl[gi]

                    for gi, (jp, p) in enumerate(groups):
                        off = (jp - 4 * J) * 128 if jp >= 4 * J else 0
                        diag = jp >= 4 * J
                        spar = gi % 2
                        ps = psAB[spar]
                        P = ppair[spar]
                        for b in (0, 1):
                            c2 = 2 * p + b
                            w0 = ()
                            if b == 0:
                                w0 = _wt(expd[gi - 2] if gi >= 2 else
                                         free["psA" if spar == 0 else "psB"],
                                         tok_rope if gi == 0 else None,
                                         t_k4 if (gi == 0 and g4 == 0 and J == 0)
                                         else None)
                            inc = ("PE", 1) if (b == 1 and not diag) else None
                            smm[gi] = mm(ps[:, b * 512 + off:(b + 1) * 512],
                                         kvar[:, jp * 512 + c2 * 128:
                                              jp * 512 + c2 * 128 + 128],
                                         q0[:, J * 512 + off:(J + 1) * 512],
                                         True, not diag, w0, inc) or smm[gi]
                        if diag:
                            for b in (0, 1):
                                c2 = 2 * p + b
                                inc = ("PE", 1) if b == 1 else None
                                smm[gi] = mm(
                                    ps[:, b * 512 + off: b * 512 + off + 128],
                                    idb[:, :],
                                    trib[:, c2 * 128:(c2 + 1) * 128],
                                    False, True,
                                    _wt(t_tri, t_idb) if gi == 0 else (),
                                    inc) or smm[gi]
                        ps3 = ps.ap().rearrange("p (b m) -> p b m", b=2)
                        P3 = P.ap().rearrange("p (b m) -> p b m", b=2)
                        expd[gi] = ir.emit("act", (lambda e, i=ps3[:, :, off:512],
                                                   o=P3[:, :, off:512]:
                                                   e.activation(o, i, AF.Exp,
                                                                bias=b16[:, :],
                                                                scale=SCALE)),
                                           _wt(smm[gi],
                                               avl[gi - 2] if gi >= 2 else
                                               free["p0" if spar == 0 else "p1"],
                                               t_consts if gi == 0 else None),
                                           inc=("ACT", 1))
                        if gi >= 1:
                            emit_avl(gi - 1, False)
                    emit_avl(ng - 1, True)
                    tok_attn_prev = smm[ng - 1]
                    free["psB"] = expd[ng - 1]
                    free["psA"] = expd[ng - 2]
                    free["p0"] = avl[ng - 1]
                    free["p1"] = avl[ng - 1]
                    if J == 1:
                        free["kvar"] = avl[ng - 1]
                    hJ = h * 2 + J
                    ir.emit("dve", lambda e: e.tensor_copy(
                        ra[:, :], psD[0:1, :]), _wt(avl[ng - 1], rdma_prev))
                    tok_ctxe = ir.emit("dve", (lambda e, h=h, J=J:
                                               e.tensor_copy(
                                                   ctx[:, h * 1024 + J * 512:
                                                       h * 1024 + J * 512 + 512],
                                                   psC[:, :])),
                                       (), inc=("DVE", 1))
                    rdma_prev = dma("sp", lall[hJ:hJ + 1, :], ra[:, :],
                                    _wt(tok_ctxe))

        def _linv_fn(e):
            with nc.allow_low_precision(reason="1/l bf16 broadcast operand"):
                return e.reciprocal(linv[:, :], lall[:, :])
        tok_linv = ir.emit("dve", _linv_fn, _wt(tok_ctxe, rdma_prev),
                           inc=("DVE", 1))
        bcast = [None, None]
        cmult = [None, None]
        rstg = [None, None]
        psCD = (psC, psD)
        rsbs = (rsb0, rsb1)
        for k in range(64):
            kp = k % 2
            bank = psCD[kp]
            rstg[kp] = dma("sp", rsbs[kp][:, :], linv[k:k + 1, :],
                           _wt(tok_linv if k <= 1 else bcast[kp]))
            bcast[kp] = mm(bank[:, :], onesbb[:, :], rsbs[kp][:, :], True, True,
                           _wt(rstg[kp], cmult[kp]), inc=("PE", 1))
            cmult[kp] = ir.emit("dve", (lambda e, k=k, b=bank: e.tensor_tensor(
                ctx[:, k * 512:(k + 1) * 512], ctx[:, k * 512:(k + 1) * 512],
                b[:, :], ALU.mult)), _wt(bcast[kp]), inc=("DVE", 1))
        tok_ctx_all = cmult[1]

        comb = [None, None]
        outd = [None, None]
        for m in range(32):
            par = m % 2
            src = wo.ap()[m * 4096:(m + 1) * 4096, :].rearrange("(c p) m -> p c m",
                                                                p=128)
            t_w = dma("gp", wbuf3[par], src, _wt(free["wb" + str(par)]), "dW")
            tok_s = None
            for half in (0, 1):
                u = 2 * m + half
                bp = u % 2
                ps = psAB[bp]
                pname = "psA" if bp == 0 else "psB"
                for c in range(32):
                    w0 = _wt(t_w, free[pname], tok_ctx_all) if c == 0 else ()
                    inc = ("PE", 1) if c == 31 else None
                    tok_s = mm(ps[:, 0:512], wbuf[par][:, c * 128:(c + 1) * 128],
                               ctx[:, c * 1024 + half * 512:
                                   c * 1024 + half * 512 + 512],
                               c == 0, c == 31, w0, inc) or tok_s
                first = _wt(g_free[0], g_free[1], tok_ctx_all,
                            tok_attn_prev) if u <= 1 else []
                t_g = dma("sp", gld[bp], gbuf.ap()[m * 128:(m + 1) * 128,
                                                   half * 512:(half + 1) * 512],
                          _wt(comb[bp]) + first)
                t_x = dma("sp", xld[bp], xT.ap()[m * 128:(m + 1) * 128,
                                                 half * 512:(half + 1) * 512], ())
                ir.emit("dve", (lambda e, s=ps, b=bp: e.tensor_tensor(
                    tsb[b], s[:, 0:512], gld[b], ALU.mult)),
                    _wt(tok_s, t_g, t_x, outd[bp]))
                comb[bp] = ir.emit("dve", (lambda e, b=bp: e.tensor_tensor(
                    tsb[b], tsb[b], xld[b], ALU.add)), (), inc=("DVE", 1))
                free[pname] = comb[bp]
                outd[bp] = ir.emit("sp", (lambda e, b=bp, m=m, half=half:
                                          e.dma_start(out=outT.ap()[
                                              m * 128:(m + 1) * 128,
                                              half * 512:(half + 1) * 512],
                                              in_=tsb[b])),
                                   _wt(comb[bp]), inc=("dO", 16))
            free["wb" + str(par)] = tok_s

        ir.emit("sp", lambda e: e.nop(), [("dO", ir.val("dO"))])
        ir.emit("gp", lambda e: e.nop(), [("dO", ir.val("dO"))])

        with nc.Block() as blk:
            def run(e, eng):
                for ws, fn, inc in ir.items[eng]:
                    for s, v in ws:
                        e.wait_ge(sems[s], v)
                    inst = fn(e)
                    if inc is not None:
                        inst.then_inc(sems[inc[0]], inc[1])

            @blk.tensor
            def _(e):
                run(e, "pe")

            @blk.scalar
            def _(e):
                run(e, "act")

            @blk.vector
            def _(e):
                run(e, "dve")

            @blk.sync
            def _(e):
                run(e, "sp")

            @blk.gpsimd
            def _(e):
                run(e, "gp")

    return nc


def _rope_tables():
    import jax
    with jax.default_device(jax.devices("cpu")[0]):
        import jax.numpy as jnp
        DH = 128
        ROPE_THETA = 500000.0
        ROPE_FACTOR = 32.0
        HI_FREQ_FACTOR = 4.0
        LO_FREQ_FACTOR = 1.0
        ORIG_CTX = 8192
        inv_freq = 1.0 / ROPE_THETA ** (jnp.arange(0, DH, 2, dtype=jnp.float32) / DH)
        wavelen = 2.0 * math.pi / inv_freq
        cut_lo = ORIG_CTX / HI_FREQ_FACTOR
        cut_hi = ORIG_CTX / LO_FREQ_FACTOR
        inv_freq = jnp.where(wavelen > cut_hi, inv_freq / ROPE_FACTOR, inv_freq)
        smooth = (ORIG_CTX / wavelen - LO_FREQ_FACTOR) / (HI_FREQ_FACTOR - LO_FREQ_FACTOR)
        smoothed = (1.0 - smooth) * inv_freq / ROPE_FACTOR + smooth * inv_freq
        is_medium = ~(wavelen < cut_hi) & ~(wavelen > cut_lo)
        inv_freq = jnp.where(is_medium, smoothed, inv_freq)
        pos = jnp.arange(4096, dtype=jnp.float32)
        freqs = pos[:, None] * inv_freq[None, :]
        emb = jnp.concatenate([freqs, freqs], axis=-1)
        cos = np.asarray(jnp.cos(emb), dtype=np.float32)
        sin = np.asarray(jnp.sin(emb), dtype=np.float32)
    return cos, sin


def kernel(L0_attn_x, L0_attn_Wq, L0_attn_Wk, L0_attn_Wv, L0_attn_Wo,
           L0_ffn_W_base, L0_ffn_b_base, L0_ffn_A_w, L0_ffn_B_w,
           L0_ffn_router_w, L0_ffn_cores):
    x = np.ascontiguousarray(np.asarray(L0_attn_x, dtype=np.float32))
    f32 = lambda a: np.ascontiguousarray(np.asarray(a, dtype=np.float32))
    bfc = lambda a: np.ascontiguousarray(np.asarray(a).astype(BF))

    def blocks(w):
        w = f32(w)
        nb = w.shape[0] // 128
        return np.ascontiguousarray(
            w.reshape(nb, 128, 4096).transpose(0, 2, 1).reshape(nb * 4096, 128)
            .astype(BF))

    wq = blocks(L0_attn_Wq)
    wkv = blocks(np.concatenate([f32(L0_attn_Wk), f32(L0_attn_Wv)], axis=0))
    wo = blocks(L0_attn_Wo)
    wb = blocks(L0_ffn_W_base)
    art = np.ascontiguousarray(
        np.concatenate([f32(L0_ffn_A_w), np.zeros((16, 4096), np.float32),
                        f32(L0_ffn_router_w)], axis=0).T)
    bwt = np.ascontiguousarray(f32(L0_ffn_B_w).T)
    coresst = np.ascontiguousarray(f32(L0_ffn_cores).reshape(128, 16))
    bb = np.ascontiguousarray(f32(L0_ffn_b_base).reshape(32, 128).T)
    identb = np.eye(128, dtype=BF)
    identf = np.eye(128, dtype=np.float32)
    rep16 = np.zeros((16, 128), dtype=BF)
    for r in range(16):
        for e_ in range(8):
            rep16[r, e_ * 16 + r] = 1
    sel8 = np.zeros((8, 128), dtype=BF)
    for e_ in range(8):
        sel8[e_, e_ * 16:(e_ + 1) * 16] = 1

    cos, sin = _rope_tables()
    sinS = sin.T.copy()
    sinS[0:64, :] *= -1.0
    cosTfull = np.ascontiguousarray(cos.T)

    if "nc" not in _NC_CACHE:
        _NC_CACHE["nc"] = build_nc()
    nc = _NC_CACHE["nc"]

    in_maps = []
    owns = []
    for core in range(8):
        b, c = divmod(core, 4)
        own = np.arange(1024) * 4 + c
        owns.append((b, own))
        tri = np.zeros((128, 512), dtype=BF)
        for c2 in range(4):
            tri[:, c2 * 128:(c2 + 1) * 128] = ((np.triu(
                np.ones((128, 128), dtype=np.float32), 0 if c2 <= c else 1
            ) - 1.0) * 1e6).astype(BF)
        in_maps.append({
            "xT": np.ascontiguousarray(x[b].T[:, own]),
            "wq": wq, "wkv": wkv, "wo": wo, "wb": wb,
            "art": art, "bwt": bwt, "coresst": coresst, "bb": bb,
            "cosT": np.ascontiguousarray(cosTfull[:, own]),
            "sinST": np.ascontiguousarray(sinS[:, own]),
            "tri": tri, "identb": identb, "identf": identf,
            "rep16": rep16, "sel8": sel8,
        })

    res = run_bass_kernel_spmd(nc, in_maps, core_ids=list(range(8)))

    out = np.empty((2, 4096, 4096), dtype=np.float32)
    for core in range(8):
        b, own = owns[core]
        out[b, own, :] = res.results[core]["outT"].T
    return out
